# version 02 (final)
# baseline (speedup 1.0000x reference)
import numpy as np
import ml_dtypes
from contextlib import ExitStack

import concourse.bass as bass
import concourse.mybir as mybir
from concourse.bass_utils import run_bass_kernel_spmd

F32 = mybir.dt.float32
BF16 = mybir.dt.bfloat16
I32 = mybir.dt.int32
AF = mybir.ActivationFunctionType
ALU = mybir.AluOpType
AX = mybir.AxisListType

BF = ml_dtypes.bfloat16

T = 2048
D = 1024
NT = T // 128
ND = D // 128

ENGS = ("sync", "scalar", "vector", "tensor", "gpsimd")


class Prog:

    def __init__(self, nc, sem):
        self.nc = nc
        self.S = sem
        self.q = {e: [] for e in ENGS}
        self.total = 0
        self.covered = {e: 0 for e in ENGS}

    def emit(self, eng, fn, inc=1):
        need = self.total
        wait = None
        if self.covered[eng] < need:
            wait = need
            self.covered[eng] = need
        self.q[eng].append((wait, fn, inc))
        self.total += inc
        if eng == "tensor":
            self.covered[eng] += inc

    def dma(self, dst, src, eng="sync", **kw):
        self.emit(eng, lambda e, o=dst, i=src, k=kw: e.dma_start(out=o, in_=i, **k), 16)

    def mm(self, out, lhsT, rhs, start=True, stop=True, skip=False):
        self.emit("tensor", lambda e, o=out, l=lhsT, r=rhs, s1=start, s2=stop, sk=skip:
                  e.matmul(o, l, r, start=s1, stop=s2, skip_group_check=sk), 1)

    def trans(self, out, in_, ident):
        self.emit("tensor", lambda e, o=out, i=in_, d=ident: e.transpose(o, i, d), 1)

    def act(self, out, in_, func, bias=0.0, scale=1.0):
        if isinstance(bias, float) and bias != 0.0:
            bias = self.imm(bias)
        self.emit("scalar", lambda e, o=out, i=in_, f=func, b=bias, s=scale:
                  e.activation(o, i, f, bias=b, scale=s), 1)

    def scopy(self, out, in_):
        self.emit("scalar", lambda e, o=out, i=in_: e.copy(o, i), 1)

    def vcopy(self, out, in_, eng="vector"):
        self.emit(eng, lambda e, o=out, i=in_: e.tensor_copy(o, i), 1)

    def tt(self, out, a, b, op, eng="vector"):
        self.emit(eng, lambda e, o=out, x=a, y=b, p=op: e.tensor_tensor(o, x, y, p), 1)

    def ts(self, out, in0, s1, s2, op0, op1=None, eng="vector"):
        if op1 is None:
            self.emit(eng, lambda e, o=out, i=in0, a=s1, p=op0:
                      e.tensor_single_scalar(o, i, a, p), 1)
        else:
            self.emit(eng, lambda e, o=out, i=in0, a=s1, b=s2, p=op0, q=op1:
                      e.tensor_scalar(o, i, a, b, p, q), 1)

    def stt(self, out, in0, scalar, in1, op0, op1, eng="vector"):
        self.emit(eng, lambda e, o=out, i=in0, s=scalar, j=in1, p=op0, q=op1:
                  e.scalar_tensor_tensor(o, i, s, j, p, q), 1)

    def red(self, out, in_, op="max", negate=False, eng="vector"):
        if op == "max":
            self.emit(eng, lambda e, o=out, i=in_, n=negate:
                      e.reduce_max(o, i, axis=AX.X, negate=n), 1)
        else:
            self.emit(eng, lambda e, o=out, i=in_, n=negate:
                      e.reduce_sum(o, i, axis=AX.X, negate=n), 1)

    def smul(self, out, in_, m):
        self.emit("scalar", lambda e, o=out, i=in_, mm_=m: e.mul(o, i, mm_), 1)

    def recip(self, out, in_):
        self.emit("vector", lambda e, o=out, i=in_: e.reciprocal(o, i), 1)

    def memset(self, ap, v, eng="vector"):
        self.emit(eng, lambda e, a=ap, x=v: e.memset(a, x), 1)

    def sel(self, out, mask, tv, fv):
        self.emit("vector", lambda e, o=out, m=mask, a=tv, b=fv: e.select(o, m, a, b), 1)

    def build(self, block, final="sync"):
        total = self.total
        self.q[final].append((total, None, 0))
        S = self.S
        for eng in ENGS:
            ops = self.q[eng]
            if not ops:
                continue

            def make(eng_ops):
                def f(e):
                    for (wait, fn, inc) in eng_ops:
                        if wait is not None:
                            e.wait_ge(S, wait)
                        if fn is not None:
                            ins = fn(e)
                            if inc:
                                ins.then_inc(S, inc)
                return f
            getattr(block, eng)(make(ops))


def make_consts():
    c = {}
    c["ident_bf"] = np.eye(128, dtype=BF)
    c["ident_f"] = np.eye(128, dtype=np.float32)
    r = np.arange(128)
    c["tri01_bf"] = (r[None, :] <= r[:, None]).astype(BF)
    jj = np.arange(16)
    c["ut16_f"] = (jj[:, None] <= jj[None, :]).astype(np.float32)
    bt = ((r[:, None] // 16 == r[None, :] // 16) & (r[:, None] <= r[None, :]))
    c["blocktriT_f"] = bt.astype(np.float32)
    c["chunksum_f"] = (r[:, None] // 16 == np.arange(8)[None, :]).astype(np.float32)
    m3 = (np.arange(128)[None, :] <= np.arange(128)[:, None] + 1)
    c["l3a_m01_f"] = (~m3).astype(np.float32)
    c["l3a_madd_f"] = np.where(m3, np.float32(-3.2e10), np.float32(0.0)).astype(np.float32)
    m3b = np.zeros((128, 128), dtype=bool); m3b[127, 0] = True
    c["l3a_m01b_f"] = (~m3b).astype(np.float32)
    c["l3a_maddb_f"] = np.where(m3b, np.float32(-3.2e10), np.float32(0.0)).astype(np.float32)
    cm = np.zeros((128, 8 * 1024), dtype=BF)
    for cc in range(8):
        for j in range(8):
            cm[:, cc * 1024 + j * 128 + cc * 16: cc * 1024 + j * 128 + (cc + 1) * 16] = 1
    c["gla_cmask"] = cm
    rm = np.zeros((128, 8), dtype=np.float32)
    for cc in range(8):
        rm[cc * 16:(cc + 1) * 16, cc] = 1
    c["gla_rmask"] = rm
    c["ones_bf"] = np.ones((1, 128), dtype=BF)
    c["ones_f"] = np.ones((1, 128), dtype=np.float32)
    half = 64
    inv_freq = 1.0 / 10000.0 ** (np.arange(half, dtype=np.float64) / half)
    ang = np.arange(T, dtype=np.float64)[:, None] * inv_freq[None, :]
    c["ropecosT_f"] = np.cos(ang).T.astype(np.float32).copy()
    c["ropesinT_f"] = np.sin(ang).T.astype(np.float32).copy()
    return c


CONST_SPECS = {
    "ident_bf": ([128, 128], BF16), "ident_f": ([128, 128], F32),
    "tri01_bf": ([128, 128], BF16), "ut16_f": ([16, 16], F32),
    "blocktriT_f": ([128, 128], F32), "chunksum_f": ([128, 8], F32),
    "l3a_m01_f": ([128, 128], F32), "l3a_madd_f": ([128, 128], F32),
    "l3a_m01b_f": ([128, 128], F32), "l3a_maddb_f": ([128, 128], F32),
    "ones_bf": ([1, 128], BF16), "ones_f": ([1, 128], F32),
    "gla_cmask": ([128, 8 * 1024], BF16), "gla_rmask": ([128, 8], F32),
    "ropecosT_f": ([64, 2048], F32), "ropesinT_f": ([64, 2048], F32),
}


class Ctx:

    def __init__(self, name):
        self.nc = bass.Bass("TRN2", target_bir_lowering=False)
        self.stack = ExitStack()
        self.inputs = {}
        self.outputs = {}

    def dram_in(self, name, shape, dtype=F32):
        ap = self.nc.dram_tensor(name, list(shape), dtype, kind="ExternalInput").ap()
        self.inputs[name] = ap
        return ap

    def dram_out(self, name, shape, dtype=F32):
        ap = self.nc.dram_tensor(name, list(shape), dtype, kind="ExternalOutput").ap()
        self.outputs[name] = ap
        return ap

    def sb(self, name, shape, dtype, stack=None):
        return (stack or self.stack).enter_context(
            self.nc.sbuf_tensor(name, list(shape), dtype))

    def sub(self):
        return ExitStack()

    def ps(self, name, shape, dtype=F32):
        return self.stack.enter_context(self.nc.psum_tensor(name, list(shape), dtype))

    def start(self):
        self.sem = self.stack.enter_context(self.nc.semaphore("S"))
        self.p = Prog(self.nc, self.sem)
        self._imm_pool = self.sb("imm_pool", [128, 32], F32)
        self._imm_map = {}

        def imm(v):
            v = float(v)
            if v not in self._imm_map:
                idx = len(self._imm_map)
                assert idx < 32
                self.p.memset(self._imm_pool[:, idx:idx + 1], v)
                self._imm_map[v] = self._imm_pool[:, idx:idx + 1]
            return self._imm_map[v]
        self.p.imm = imm
        return self.p

    def load_consts(self, names):
        self.c = {}
        for n in names:
            shape, dt = CONST_SPECS[n]
            ap = self.dram_in("c_" + n, shape, dt)
            t = self.sb("cs_" + n, shape, dt)
            self.p.dma(t[:, :], ap)
            self.c[n] = t

    def finish(self, block):
        self.p.build(block)


def const_inmap(consts, names):
    return {"c_" + n: np.ascontiguousarray(consts[n]) for n in names}


def emit_ln(ctx, x_sb, i, xn32_out, xnbf_sb, scr):
    p = ctx.p
    xt = x_sb[:, i * D:(i + 1) * D]
    p.red(scr["m"][:, 0:1], xt, op="sum")
    p.act(scr["m"][:, 1:2], scr["m"][:, 0:1], AF.Copy, scale=1.0 / D)
    p.ts(xn32_out[:, :], xt, scr["m"][:, 1:2], None, ALU.subtract)
    p.tt(scr["sq"][:, :], xn32_out[:, :], xn32_out[:, :], ALU.mult)
    p.red(scr["m"][:, 2:3], scr["sq"][:, :], op="sum")
    p.act(scr["m"][:, 3:4], scr["m"][:, 2:3], AF.Ln, scale=1.0 / D, bias=1e-5)
    p.act(scr["m"][:, 4:5], scr["m"][:, 3:4], AF.Exp, scale=-0.5)
    p.ts(xn32_out[:, :], xn32_out[:, :], scr["m"][:, 4:5], None, ALU.mult)
    p.vcopy(xnbf_sb[:, i * D:(i + 1) * D], xn32_out[:, :])


def emit_preamble(ctx, x_hbm, want_xnbf=False, routers=None, keys_wbc=None, xnT=None, xnbf=None, router_dsts=None):
    p, c = ctx.p, ctx.c
    if xnbf is None and want_xnbf:
        xnbf = ctx.sb("xnbf", [128, NT * D], BF16)
    if xnT is None:
        xnT = ctx.sb("xnT", [128, ND * T], BF16)
    sub = ctx.sub()
    xt_scr = ctx.sb("xt_scr", [128, D], F32, sub)
    xn32 = ctx.sb("xn32_scr", [128, D], F32, sub)
    xnbf_s = ctx.sb("xnbf_scr", [128, D], BF16, sub)
    xnT32s = ctx.sb("xnT32_scr", [128, 128], F32, sub)
    scr = {"m": ctx.sb("ln_m", [128, 8], F32, sub), "sq": ctx.sb("ln_sq", [128, D], F32, sub)}
    ps_tf = ctx.ps_tf
    ps_tb = ctx.ps_tb
    for i in range(NT):
        p.dma(xt_scr[:, :], x_hbm[i * 128:(i + 1) * 128, :])
        xt = xt_scr
        p.red(scr["m"][:, 0:1], xt[:, :], op="sum")
        p.act(scr["m"][:, 1:2], scr["m"][:, 0:1], AF.Copy, scale=1.0 / D)
        p.ts(xn32[:, :], xt[:, :], scr["m"][:, 1:2], None, ALU.subtract)
        p.tt(scr["sq"][:, :], xn32[:, :], xn32[:, :], ALU.mult)
        p.red(scr["m"][:, 2:3], scr["sq"][:, :], op="sum")
        p.act(scr["m"][:, 3:4], scr["m"][:, 2:3], AF.Ln, scale=1.0 / D, bias=1e-5)
        p.act(scr["m"][:, 4:5], scr["m"][:, 3:4], AF.Exp, scale=-0.5)
        p.ts(xn32[:, :], xn32[:, :], scr["m"][:, 4:5], None, ALU.mult)
        if keys_wbc is not None:
            wbc, keys_sb = keys_wbc
            p.tt(scr["sq"][:, :], xn32[:, :], wbc[:, :], ALU.mult)
            p.red(keys_sb[:, i:i + 1], scr["sq"][:, :], op="sum")
        dstbf = xnbf[:, i * D:(i + 1) * D] if want_xnbf else xnbf_s[:, :]
        p.vcopy(dstbf, xn32[:, :])
        for j in range(ND):
            if routers:
                p.trans(ps_tf[:, 0:128], xn32[:, j * 128:(j + 1) * 128], c["ident_f"][:, :])
                p.scopy(xnT32s[:, :], ps_tf[:, 0:128])
                roff = 0
                for (rhs, M, dst) in routers:
                    p.mm(ctx.ps_r[:, roff:roff + M], xnT32s[:, :],
                         rhs[:, j * M:(j + 1) * M],
                         start=(j == 0), stop=(j == ND - 1))
                    roff += M
                p.vcopy(xnT[:, j * T + i * 128: j * T + (i + 1) * 128], xnT32s[:, :])
            else:
                src = dstbf[:, j * 128:(j + 1) * 128]
                p.trans(ps_tb[:, 0:128], src, c["ident_bf"][:, :])
                p.scopy(xnT[:, j * T + i * 128: j * T + (i + 1) * 128], ps_tb[:, 0:128])
        if routers:
            for (dst, M, roff) in router_dsts:
                p.scopy(dst[:, i * M:(i + 1) * M], ctx.ps_r[:, roff:roff + M])
    sub.close()
    return xnbf, xnT


def emit_load_w_nat(ctx, w_hbm, k0, nout, wband_f32, wband_bf, col0=0, ncols=None):
    p = ctx.p
    if ncols is None:
        ncols = nout
    p.dma(wband_f32[:, 0:ncols], w_hbm[k0 * 128:(k0 + 1) * 128, col0:col0 + ncols])
    p.vcopy(wband_bf[:, 0:ncols], wband_f32[:, 0:ncols])


def emit_transpose_w(ctx, w_hbm, nout, nin, dst_bf, scr_f32, scr_bf):
    p, c = ctx.p, ctx.c
    ps_tb = ctx.ps_tb
    for m in range((nout + 127) // 128):
        rows = min(128, nout - m * 128)
        p.dma(scr_f32[0:rows, 0:nin], w_hbm[m * 128:m * 128 + rows, :])
        p.vcopy(scr_bf[0:rows, 0:nin], scr_f32[0:rows, 0:nin])
        for j in range(nin // 128):
            p.trans(ps_tb[:, 0:rows], scr_bf[0:rows, j * 128:(j + 1) * 128], c["ident_bf"][:, :])
            p.scopy(dst_bf[:, j * nout + m * 128: j * nout + m * 128 + rows], ps_tb[:, 0:rows])


def emit_flash_head(ctx, qT_h, kT_h, v_getter, hd, scale, outT_writer,
                    mask_mode, ps_S, ps_O, P_sb, PT_sb, stat):
    raise NotImplementedError


def build_l2():
    ctx = Ctx("l2")
    nc = ctx.nc
    x_hbm = ctx.dram_in("x", [T, D])
    wqvk_hbm = ctx.dram_in("wqvk", [D, 3 * D])
    wo_hbm = ctx.dram_in("wo", [D, D])
    wf_hbm = ctx.dram_in("wf", [D, D])
    scale_hbm = ctx.dram_in("fscale", [1, D])
    bias_hbm = ctx.dram_in("fbias", [1, D])
    xo_hbm = ctx.dram_out("xout", [T, D])

    ctx.ps_a = ctx.ps("ps_a", [128, 2048], F32)
    ctx.ps_c = ctx.ps("ps_c", [128, 512], F32)
    ctx.ps_tb = ctx.ps("ps_tb", [128, 128], BF16)
    ctx.ps_tf = ctx.ps("ps_tf", [128, 128], F32)

    p = ctx.start()
    ctx.load_consts(["ident_bf", "ident_f", "tri01_bf", "ones_bf", "ones_f"])
    c = ctx.c

    _, xnT = emit_preamble(ctx, x_hbm)

    H = 16
    attn_oT = ctx.sb("attn_oT", [128, ND * T], BF16)
    with ctx.sub() as sub:
        qvkT = ctx.sb("qvkT", [128, 16 * T], BF16, sub)
        with ctx.sub() as sub2:
            wband_f = ctx.sb("wband_f", [128, 3 * D], F32, sub2)
            wband_b = [ctx.sb(f"wband_b{k}", [128, 3 * D], BF16, sub2) for k in range(ND)]
            for k in range(ND):
                emit_load_w_nat(ctx, wqvk_hbm, k, 3 * D, wband_f, wband_b[k])
            for g, col0 in ((0, 0), (1, 2 * D)):
                for m in range(ND):
                    for tchunk in range(4):
                        for k in range(ND):
                            p.mm(ctx.ps_c[:, 0:512],
                                 wband_b[k][:, col0 + m * 128: col0 + (m + 1) * 128],
                                 xnT[:, k * T + tchunk * 512: k * T + (tchunk + 1) * 512],
                                 start=(k == 0), stop=(k == ND - 1))
                        p.scopy(qvkT[:, (g * ND + m) * T + tchunk * 512:
                                      (g * ND + m) * T + (tchunk + 1) * 512],
                                ctx.ps_c[:, 0:512])
        v_sb = ctx.sb("v_sb", [128, NT * D], BF16, sub)
        with ctx.sub() as sub2:
            vband_f = ctx.sb("vband_f", [128, D], F32, sub2)
            vband_b = [ctx.sb(f"vband_b{k}", [128, D], BF16, sub2) for k in range(ND)]
            for k in range(ND):
                emit_load_w_nat(ctx, wqvk_hbm, k, 3 * D, vband_f, vband_b[k],
                                col0=D, ncols=D)
            for i in range(NT):
                for nchunk in range(2):
                    for k in range(ND):
                        p.mm(ctx.ps_c[:, 0:512],
                             xnT[:, k * T + i * 128: k * T + (i + 1) * 128],
                             vband_b[k][:, nchunk * 512:(nchunk + 1) * 512],
                             start=(k == 0), stop=(k == ND - 1))
                    p.scopy(v_sb[:, i * D + nchunk * 512: i * D + (nchunk + 1) * 512],
                            ctx.ps_c[:, 0:512])

        P_sb = ctx.sb("P_sb", [128, 2048], BF16, sub)
        PT_sb = ctx.sb("PT_sb", [128, 128], BF16, sub)
        stat = ctx.sb("stat", [128, 8], F32, sub)
        sc = 1.0 / np.sqrt(np.float32(D))
        for h in range(H):
            ot, orow = h // 2, 64 * (h % 2)
            qT = lambda i, ot=ot, orow=orow: qvkT[orow:orow + 64,
                                                  ot * T + i * 128: ot * T + (i + 1) * 128]
            kTc = lambda k0, n, ot=ot, orow=orow: qvkT[orow:orow + 64,
                                                       (ND + ot) * T + k0: (ND + ot) * T + k0 + n]
            for i in range(NT):
                klen = (i + 1) * 128
                for k0 in range(0, klen, 512):
                    n = min(512, klen - k0)
                    p.mm(ctx.ps_a[:, k0:k0 + n], qT(i), kTc(k0, n), start=True, stop=True)
                p.red(stat[:, 0:1], ctx.ps_a[:, 0:klen], op="max", negate=True)
                p.act(stat[:, 1:2], stat[:, 0:1], AF.Copy, scale=sc)
                p.act(P_sb[:, 0:klen], ctx.ps_a[:, 0:klen], AF.Exp, bias=stat[:, 1:2], scale=sc)
                p.tt(P_sb[:, i * 128:klen], P_sb[:, i * 128:klen], c["tri01_bf"][:, :], ALU.mult)
                p.red(stat[:, 2:3], P_sb[:, 0:klen], op="sum")
                p.recip(stat[:, 3:4], stat[:, 2:3])
                p.ts(P_sb[:, 0:klen], P_sb[:, 0:klen], stat[:, 3:4], None, ALU.mult)
                for kt in range(i + 1):
                    p.trans(ctx.ps_tb[:, 0:128], P_sb[:, kt * 128:(kt + 1) * 128],
                            c["ident_bf"][:, :])
                    p.scopy(PT_sb[:, 0:128], ctx.ps_tb[:, 0:128])
                    p.mm(ctx.ps_c[0:64, 0:128],
                         v_sb[:, kt * D + h * 64: kt * D + (h + 1) * 64],
                         PT_sb[:, 0:128], start=(kt == 0), stop=(kt == i))
                p.scopy(attn_oT[orow:orow + 64, ot * T + i * 128: ot * T + (i + 1) * 128],
                        ctx.ps_c[0:64, 0:128])

    wo_f = ctx.sb("wo_f", [128, D], F32)
    wo_b = [ctx.sb(f"wo_b{k}", [128, D], BF16) for k in range(ND)]
    for k in range(ND):
        emit_load_w_nat(ctx, wo_hbm, k, D, wo_f, wo_b[k])
    ws_b = ctx.sb("ws_b", [128, D], BF16)
    wsT = ctx.sb("wsT", [128, ND * D], BF16)
    for m in range(ND):
        p.dma(wo_f[:, :], wf_hbm[m * 128:(m + 1) * 128, :])
        p.ts(ws_b[:, :], wo_f[:, :], 0.0, 2.0, ALU.is_ge, ALU.mult)
        p.ts(ws_b[:, :], ws_b[:, :], 1.0, None, ALU.subtract)
        for j in range(ND):
            p.trans(ctx.ps_tb[:, 0:128], ws_b[:, j * 128:(j + 1) * 128], c["ident_bf"][:, :])
            p.scopy(wsT[:, j * D + m * 128: j * D + (m + 1) * 128], ctx.ps_tb[:, 0:128])
    srow = ctx.sb("srow", [1, 2 * D], F32)
    p.dma(srow[0:1, 0:D], scale_hbm[0:1, :])
    p.dma(srow[0:1, D:2 * D], bias_hbm[0:1, :])
    sc_bc = ctx.sb("sc_bc", [128, D], F32)
    bi_bc = ctx.sb("bi_bc", [128, D], F32)
    for nchunk in range(2):
        p.mm(ctx.ps_c[:, 0:512], c["ones_f"][0:1, 0:128],
             srow[0:1, nchunk * 512:(nchunk + 1) * 512], start=True, stop=True)
        p.scopy(sc_bc[:, nchunk * 512:(nchunk + 1) * 512], ctx.ps_c[:, 0:512])
        p.mm(ctx.ps_c[:, 0:512], c["ones_f"][0:1, 0:128],
             srow[0:1, D + nchunk * 512: D + (nchunk + 1) * 512], start=True, stop=True)
        p.scopy(bi_bc[:, nchunk * 512:(nchunk + 1) * 512], ctx.ps_c[:, 0:512])
    tmp = ctx.sb("ffn_tmp", [128, 512], F32)
    xt = ctx.sb("xt_res", [128, D], F32)
    for i in range(NT):
        p.dma(xt[:, :], x_hbm[i * 128:(i + 1) * 128, :])
        for nchunk in range(2):
            cs = slice(nchunk * 512, (nchunk + 1) * 512)
            for k in range(ND):
                p.mm(ctx.ps_c[:, 0:512],
                     attn_oT[:, k * T + i * 128: k * T + (i + 1) * 128],
                     wo_b[k][:, cs], start=(k == 0), stop=(k == ND - 1))
            p.tt(xt[:, cs], xt[:, cs], ctx.ps_c[:, 0:512], ALU.add)
            for k in range(ND):
                p.mm(ctx.ps_c[:, 0:512],
                     xnT[:, k * T + i * 128: k * T + (i + 1) * 128],
                     wsT[:, k * D + nchunk * 512: k * D + (nchunk + 1) * 512],
                     start=(k == 0), stop=(k == ND - 1))
            p.tt(tmp[:, :], ctx.ps_c[:, 0:512], sc_bc[:, cs], ALU.mult)
            p.tt(tmp[:, :], tmp[:, :], bi_bc[:, cs], ALU.add)
            p.ts(tmp[:, :], tmp[:, :], 100.0, -100.0, ALU.min, ALU.max)
            p.tt(xt[:, cs], xt[:, cs], tmp[:, :], ALU.add)
        p.dma(xo_hbm[i * 128:(i + 1) * 128, :], xt[:, :], eng="gpsimd")

    with nc.Block() as block:
        ctx.finish(block)
    ctx.stack.close()
    return ctx


def run_l2(x, wqvk, wo, wf, fscale, fbias, consts, cores):
    ctx = build_l2()
    cnames = ["ident_bf", "ident_f", "tri01_bf", "ones_bf", "ones_f"]
    maps = []
    for ci in cores:
        m = {"x": np.ascontiguousarray(x[ci]), "wqvk": wqvk, "wo": wo, "wf": wf,
             "fscale": fscale, "fbias": fbias}
        m.update(const_inmap(consts, cnames))
        maps.append(m)
    res = run_bass_kernel_spmd(ctx.nc, maps, core_ids=list(cores))
    return np.stack([r["xout"] for r in res.results])


def build_l1a():
    ctx = Ctx("l1a")
    nc = ctx.nc
    x_hbm = ctx.dram_in("x", [T, D])
    wattn_hbm = ctx.dram_in("wattn", [D, 3 * D])
    battn_row = ctx.dram_in("battn_row", [1, 3 * D])
    battn_col = ctx.dram_in("battn_col", [3 * D, 1])
    wproj_hbm = ctx.dram_in("wproj", [D, D])
    bproj_row = ctx.dram_in("bproj_row", [1, D])
    rw0_hbm = ctx.dram_in("rw0", [D, 32])
    rw1_hbm = ctx.dram_in("rw1", [D, 32])
    xa_hbm = ctx.dram_out("xa", [T, D])
    xnT_hbm = ctx.dram_out("xnT_o", [128, ND * T], BF16)
    wselT_hbm = ctx.dram_out("wselT_o", [64, T])

    ctx.ps_a = ctx.ps("ps_a", [128, 2048], F32)
    ctx.ps_c = ctx.ps("ps_c", [128, 512], F32)
    ctx.ps_tb = ctx.ps("ps_tb", [128, 128], BF16)
    ctx.ps_tf = ctx.ps("ps_tf", [128, 128], F32)
    ctx.ps_r = ctx.ps("ps_r", [128, 64], F32)

    p = ctx.start()
    ctx.load_consts(["ident_bf", "ident_f", "tri01_bf", "ones_bf", "ones_f"])
    c = ctx.c

    xnT = ctx.sb("xnT_main", [128, ND * T], BF16)
    sel_sub = ctx.sub()
    rw_sb = ctx.sb("rwsb", [128, ND * 64], F32, sel_sub)
    for r, h in ((0, rw0_hbm), (1, rw1_hbm)):
        for j in range(ND):
            p.dma(rw_sb[:, j * 64 + r * 32: j * 64 + (r + 1) * 32],
                  h[j * 128:(j + 1) * 128, :])
    logit_sb = [ctx.sb(f"logit{r}", [128, NT * 32], F32, sel_sub) for r in range(2)]
    wselT = ctx.sb("wselT", [64, T], F32, sel_sub)
    st = ctx.sb("st_sel", [128, 8], F32, sel_sub)
    sc32 = ctx.sb("sc32", [128, 32], F32, sel_sub)
    ge32 = ctx.sb("ge32", [128, 32], F32, sel_sub)
    routers = [(rw_sb, 64, None)]
    emit_preamble(ctx, x_hbm, routers=routers, xnT=xnT,
                  router_dsts=[(logit_sb[0], 32, 0), (logit_sb[1], 32, 32)])
    for j in range(ND):
        p.dma(xnT_hbm[:, j * T:(j + 1) * T], xnT[:, j * T:(j + 1) * T], eng="gpsimd")

    for r in range(2):
        for i in range(NT):
            L = logit_sb[r][:, i * 32:(i + 1) * 32]
            p.red(st[:, 0:1], L, op="max", negate=True)
            p.act(sc32[:, :], L, AF.Exp, bias=st[:, 0:1])
            p.red(st[:, 1:2], sc32[:, :], op="sum")
            p.recip(st[:, 2:3], st[:, 1:2])
            p.ts(sc32[:, :], sc32[:, :], st[:, 2:3], None, ALU.mult)
            p.red(st[:, 3:4], sc32[:, :], op="max")
            p.ts(ge32[:, :], sc32[:, :], st[:, 3:4], None, ALU.is_equal)
            p.stt(ge32[:, :], ge32[:, :], p.imm(-2.0), sc32[:, :], ALU.mult, ALU.add)
            p.red(st[:, 4:5], ge32[:, :], op="max")
            p.ts(ge32[:, :], sc32[:, :], st[:, 4:5], None, ALU.is_ge)
            p.stt(sc32[:, :], sc32[:, :], p.imm(0.5), ge32[:, :], ALU.mult, ALU.mult)
            p.trans(ctx.ps_tf[0:32, 0:128], sc32[:, :], c["ident_f"][:, :])
            p.scopy(wselT[r * 32:(r + 1) * 32, i * 128:(i + 1) * 128], ctx.ps_tf[0:32, 0:128])
    p.dma(wselT_hbm, wselT[:, :], eng="gpsimd")
    sel_sub.close()

    H = 16
    attn_oT = ctx.sb("attn_oT", [128, ND * T], BF16)
    bcol = ctx.sb("bcol", [128, 24], F32)
    for m in range(24):
        p.dma(bcol[:, m:m + 1], battn_col[m * 128:(m + 1) * 128, 0:1])

    with ctx.sub() as sub:
        qvkT = ctx.sb("qvkT", [128, 16 * T], BF16)
        v_sb = ctx.sb("v_sb", [128, NT * D], BF16)
        with ctx.sub() as sub2:
            wband_f = ctx.sb("wband_f", [128, 2 * D], F32, sub2)
            wband_b = [ctx.sb(f"wband_b{k}", [128, 2 * D], BF16, sub2) for k in range(ND)]
            for k in range(ND):
                emit_load_w_nat(ctx, wattn_hbm, k, 3 * D, wband_f, wband_b[k], col0=0, ncols=2 * D)
            for g, col0 in ((0, 0), (1, D)):
                for m in range(ND):
                    for tchunk in range(4):
                        for k in range(ND):
                            p.mm(ctx.ps_c[:, 0:512],
                                 wband_b[k][:, col0 + m * 128: col0 + (m + 1) * 128],
                                 xnT[:, k * T + tchunk * 512: k * T + (tchunk + 1) * 512],
                                 start=(k == 0), stop=(k == ND - 1))
                        p.ts(qvkT[:, (g * ND + m) * T + tchunk * 512:
                                  (g * ND + m) * T + (tchunk + 1) * 512],
                             ctx.ps_c[:, 0:512], bcol[:, g * ND + m: g * ND + m + 1],
                             None, ALU.add)
        with ctx.sub() as sub3:
            brow = ctx.sb("brow", [1, D], F32, sub3)
            p.dma(brow[0:1, :], battn_row[0:1, 2 * D:3 * D])
            vb_bc = ctx.sb("vb_bc", [128, D], F32, sub3)
            for nch in range(2):
                p.mm(ctx.ps_c[:, 0:512], c["ones_f"][0:1, 0:128],
                     brow[0:1, nch * 512:(nch + 1) * 512], start=True, stop=True)
                p.scopy(vb_bc[:, nch * 512:(nch + 1) * 512], ctx.ps_c[:, 0:512])
            vband_f = ctx.sb("vband_f", [128, D], F32, sub3)
            vband_b = [ctx.sb(f"vband_b{k}", [128, D], BF16, sub3) for k in range(ND)]
            for k in range(ND):
                emit_load_w_nat(ctx, wattn_hbm, k, 3 * D, vband_f, vband_b[k],
                                col0=2 * D, ncols=D)
            for i in range(NT):
                for nchunk in range(2):
                    for k in range(ND):
                        p.mm(ctx.ps_c[:, 0:512],
                             xnT[:, k * T + i * 128: k * T + (i + 1) * 128],
                             vband_b[k][:, nchunk * 512:(nchunk + 1) * 512],
                             start=(k == 0), stop=(k == ND - 1))
                    p.tt(v_sb[:, i * D + nchunk * 512: i * D + (nchunk + 1) * 512],
                         ctx.ps_c[:, 0:512],
                         vb_bc[:, nchunk * 512:(nchunk + 1) * 512], ALU.add)
        P_sb = ctx.sb("P_sb", [128, 2048], BF16, sub)
        PT_sb = ctx.sb("PT_sb", [128, 128], BF16, sub)
        stat = ctx.sb("stat", [128, 8], F32, sub)
        sc = 0.125
        for h in range(H):
            ot, orow = h // 2, 64 * (h % 2)
            for i in range(NT):
                klen = (i + 1) * 128
                for k0 in range(0, klen, 512):
                    n = min(512, klen - k0)
                    p.mm(ctx.ps_a[:, k0:k0 + n],
                         qvkT[orow:orow + 64, ot * T + i * 128: ot * T + (i + 1) * 128],
                         qvkT[orow:orow + 64, (ND + ot) * T + k0:(ND + ot) * T + k0 + n],
                         start=True, stop=True)
                p.red(stat[:, 0:1], ctx.ps_a[:, 0:klen], op="max", negate=True)
                p.act(stat[:, 1:2], stat[:, 0:1], AF.Copy, scale=sc)
                p.act(P_sb[:, 0:klen], ctx.ps_a[:, 0:klen], AF.Exp, bias=stat[:, 1:2], scale=sc)
                p.tt(P_sb[:, i * 128:klen], P_sb[:, i * 128:klen], c["tri01_bf"][:, :], ALU.mult)
                p.red(stat[:, 2:3], P_sb[:, 0:klen], op="sum")
                p.recip(stat[:, 3:4], stat[:, 2:3])
                p.ts(P_sb[:, 0:klen], P_sb[:, 0:klen], stat[:, 3:4], None, ALU.mult)
                for kt in range(i + 1):
                    p.trans(ctx.ps_tb[:, 0:128], P_sb[:, kt * 128:(kt + 1) * 128],
                            c["ident_bf"][:, :])
                    p.scopy(PT_sb[:, 0:128], ctx.ps_tb[:, 0:128])
                    p.mm(ctx.ps_c[0:64, 0:128],
                         v_sb[:, kt * D + h * 64: kt * D + (h + 1) * 64],
                         PT_sb[:, 0:128], start=(kt == 0), stop=(kt == i))
                p.scopy(attn_oT[orow:orow + 64, ot * T + i * 128: ot * T + (i + 1) * 128],
                        ctx.ps_c[0:64, 0:128])

    wp_f = ctx.sb("wp_f", [128, D], F32)
    wp_b = [ctx.sb(f"wp_b{k}", [128, D], BF16) for k in range(ND)]
    for k in range(ND):
        emit_load_w_nat(ctx, wproj_hbm, k, D, wp_f, wp_b[k])
    bprow = ctx.sb("bprow", [1, D], F32)
    p.dma(bprow[0:1, :], bproj_row[0:1, :])
    pb_bc = ctx.sb("pb_bc", [128, D], F32)
    for nch in range(2):
        p.mm(ctx.ps_c[:, 0:512], c["ones_f"][0:1, 0:128],
             bprow[0:1, nch * 512:(nch + 1) * 512], start=True, stop=True)
        p.scopy(pb_bc[:, nch * 512:(nch + 1) * 512], ctx.ps_c[:, 0:512])
    xt = ctx.sb("xt_res", [128, D], F32)
    for i in range(NT):
        p.dma(xt[:, :], x_hbm[i * 128:(i + 1) * 128, :])
        for nchunk in range(2):
            cs = slice(nchunk * 512, (nchunk + 1) * 512)
            for k in range(ND):
                p.mm(ctx.ps_c[:, 0:512],
                     attn_oT[:, k * T + i * 128: k * T + (i + 1) * 128],
                     wp_b[k][:, cs], start=(k == 0), stop=(k == ND - 1))
            p.tt(xt[:, cs], xt[:, cs], pb_bc[:, cs], ALU.add)
            p.tt(xt[:, cs], xt[:, cs], ctx.ps_c[:, 0:512], ALU.add)
        p.dma(xa_hbm[i * 128:(i + 1) * 128, :], xt[:, :], eng="gpsimd")

    with nc.Block() as block:
        ctx.finish(block)
    ctx.stack.close()
    return ctx


def build_l1f():
    ctx = Ctx("l1f")
    nc = ctx.nc
    xa_hbm = ctx.dram_in("xa", [T, D])
    xnT_hbm = ctx.dram_in("xnT_i", [128, ND * T], BF16)
    wselT_hbm = ctx.dram_in("wselT_i", [64, T])
    wg_h = [ctx.dram_in(f"wg{g}", [32, D, 512]) for g in range(2)]
    wu_h = [ctx.dram_in(f"wu{g}", [32, D, 512]) for g in range(2)]
    wd_h = [ctx.dram_in(f"wd{g}", [32, 512, D]) for g in range(2)]
    x2_hbm = ctx.dram_out("x2", [T, D])

    ctx.ps_c = ctx.ps("ps_c", [128, 512], F32)
    ctx.ps_g = ctx.ps("ps_g", [128, 512], F32)
    ctx.ps_bc = ctx.ps("ps_bc", [128, 512], F32)
    ctx.ps_tb = ctx.ps("ps_tb", [128, 128], BF16)

    p = ctx.start()
    ctx.load_consts(["ident_bf", "ones_f"])
    c = ctx.c

    xnT = ctx.sb("xnT", [128, ND * T], BF16)
    for j in range(ND):
        p.dma(xnT[:, j * T:(j + 1) * T], xnT_hbm[:, j * T:(j + 1) * T])
    wselrow = ctx.sb("wselrow", [1, T], F32)
    accT = ctx.sb("accT", [128, ND * T], BF16)
    p.memset(accT[:, :], 0.0)

    wgb = ctx.sb("wgb", [128, ND * 512], BF16)
    wub = ctx.sb("wub", [128, ND * 512], BF16)
    wdb = ctx.sb("wdb", [128, 4 * D], BF16)
    wstage = ctx.sb("wstage", [128, D], F32)
    gsil = ctx.sb("gsil", [128, 512], BF16)
    wbc = ctx.sb("wbc", [128, 512], F32)
    h_sb = ctx.sb("h_sb", [128, 4 * T], BF16)

    for g in range(2):
        for e in range(32):
            p.dma(wselrow[0:1, :], wselT_hbm[g * 32 + e: g * 32 + e + 1, :])
            for j in range(ND):
                p.dma(wstage[:, 0:512], wg_h[g][e, j * 128:(j + 1) * 128, :])
                p.vcopy(wgb[:, j * 512:(j + 1) * 512], wstage[:, 0:512])
                p.dma(wstage[:, 0:512], wu_h[g][e, j * 128:(j + 1) * 128, :])
                p.vcopy(wub[:, j * 512:(j + 1) * 512], wstage[:, 0:512])
            for fm in range(4):
                p.dma(wstage[:, 0:D], wd_h[g][e, fm * 128:(fm + 1) * 128, :])
                p.vcopy(wdb[:, fm * D:(fm + 1) * D], wstage[:, 0:D])
            for tchunk in range(4):
                tc = slice(tchunk * 512, (tchunk + 1) * 512)
                p.mm(ctx.ps_bc[:, 0:512], c["ones_f"][0:1, 0:128],
                     wselrow[0:1, tc], start=True, stop=True)
                p.scopy(wbc[:, 0:512], ctx.ps_bc[:, 0:512])
                for fm in range(4):
                    for j in range(ND):
                        p.mm(ctx.ps_g[:, 0:512],
                             wgb[:, j * 512 + fm * 128: j * 512 + (fm + 1) * 128],
                             xnT[:, j * T + tchunk * 512: j * T + (tchunk + 1) * 512],
                             start=(j == 0), stop=(j == ND - 1))
                    p.act(gsil[:, 0:512], ctx.ps_g[:, 0:512], AF.Silu)
                    for j in range(ND):
                        p.mm(ctx.ps_c[:, 0:512],
                             wub[:, j * 512 + fm * 128: j * 512 + (fm + 1) * 128],
                             xnT[:, j * T + tchunk * 512: j * T + (tchunk + 1) * 512],
                             start=(j == 0), stop=(j == ND - 1))
                    p.tt(h_sb[:, fm * T + tchunk * 512: fm * T + (tchunk + 1) * 512],
                         ctx.ps_c[:, 0:512], gsil[:, 0:512], ALU.mult)
                    p.tt(h_sb[:, fm * T + tchunk * 512: fm * T + (tchunk + 1) * 512],
                         h_sb[:, fm * T + tchunk * 512: fm * T + (tchunk + 1) * 512],
                         wbc[:, 0:512], ALU.mult)
                for dm in range(ND):
                    for fm in range(4):
                        p.mm(ctx.ps_c[:, 0:512],
                             wdb[:, fm * D + dm * 128: fm * D + (dm + 1) * 128],
                             h_sb[:, fm * T + tchunk * 512: fm * T + (tchunk + 1) * 512],
                             start=(fm == 0), stop=(fm == 3))
                    p.tt(accT[:, dm * T + tchunk * 512: dm * T + (tchunk + 1) * 512],
                         accT[:, dm * T + tchunk * 512: dm * T + (tchunk + 1) * 512],
                         ctx.ps_c[:, 0:512], ALU.add)
    xt = ctx.sb("xt_res", [128, D], F32)
    for i in range(NT):
        p.dma(xt[:, :], xa_hbm[i * 128:(i + 1) * 128, :])
        for j in range(ND):
            p.trans(ctx.ps_tb[:, 0:128], accT[:, j * T + i * 128: j * T + (i + 1) * 128],
                    c["ident_bf"][:, :])
            p.tt(xt[:, j * 128:(j + 1) * 128], xt[:, j * 128:(j + 1) * 128],
                 ctx.ps_tb[:, 0:128], ALU.add)
        p.dma(x2_hbm[i * 128:(i + 1) * 128, :], xt[:, :], eng="gpsimd")

    with nc.Block() as block:
        ctx.finish(block)
    ctx.stack.close()
    return ctx




L0A_PHASE = 2
L0A_SCANLEVEL = 4
L0A_NOMASK = 0
L0A_CONSTS = ["ident_bf", "ident_f", "ones_bf", "ones_f", "blocktriT_f",
              "chunksum_f", "gla_cmask", "gla_rmask"]


def build_l0a():
    ctx = Ctx("l0a")
    nc = ctx.nc
    x_hbm = ctx.dram_in("x", [T, D])
    keyw_hbm = ctx.dram_in("keyw", [D, D])
    keyb_row = ctx.dram_in("keyb_row", [1, D])
    vqw_hbm = ctx.dram_in("vqw", [2 * D, D])
    outw_hbm = ctx.dram_in("outw", [D, D])
    outb_row = ctx.dram_in("outb_row", [1, D])
    wr_row = ctx.dram_in("wr_row", [1, D])
    xa_hbm = ctx.dram_out("xa", [T, D])
    xnT_hbm = ctx.dram_out("xnT_o", [128, ND * T], BF16)
    keys_hbm = ctx.dram_out("keys_o", [128, NT])

    ctx.ps_L = ctx.ps("ps_L", [128, 1024], F32)
    ctx.ps_acc = ctx.ps("ps_acc", [128, 1024], F32)
    ctx.ps_pAB = ctx.ps("ps_pAB", [64, 128], F32)
    ctx.ps_tb = ctx.ps("ps_tb", [128, 128], BF16)
    ctx.ps_c = ctx.ps("ps_c", [128, 512], F32)
    ctx.ps_misc = ctx.ps("ps_misc", [128, 128], F32)
    ctx.ps_tf = ctx.ps_misc

    p = ctx.start()
    ctx.load_consts(L0A_CONSTS)
    c = ctx.c

    wrrow = ctx.sb("wrrow", [1, D], F32)
    p.dma(wrrow[0:1, :], wr_row[0:1, :])
    wrbc = ctx.sb("wrbc", [128, D], F32)
    for nch in range(2):
        p.mm(ctx.ps_c[:, 0:512], c["ones_f"][0:1, 0:128],
             wrrow[0:1, nch * 512:(nch + 1) * 512], start=True, stop=True)
        p.scopy(wrbc[:, nch * 512:(nch + 1) * 512], ctx.ps_c[:, 0:512])
    keys_sb = ctx.sb("keys_sb", [128, NT], F32)
    xnT = ctx.sb("xnT_main", [128, ND * T], BF16)
    emit_preamble(ctx, x_hbm, xnT=xnT, keys_wbc=(wrbc, keys_sb))
    p.dma(keys_hbm, keys_sb[:, :], eng="gpsimd")
    for j in range(ND):
        p.dma(xnT_hbm[:, j * T:(j + 1) * T], xnT[:, j * T:(j + 1) * T], eng="gpsimd")

    wT_scr_f = ctx.sb("wT_scr_f", [128, D], F32)
    wT_scr_b = ctx.sb("wT_scr_b", [128, D], BF16)
    h_sb = ctx.sb("h_gla", [128, NT * D], BF16)
    p.memset(h_sb[:, :], 0.0)
    S_sb = ctx.sb("S_gla", [128, 8 * 64], F32)
    S_blk = ctx.sb("S_gla_blk", [128, 8 * 128], BF16)
    p.memset(S_sb[:, :], 0.0)
    p.memset(S_blk[:, :], 0.0)

    with ctx.sub() as sub:
      if L0A_PHASE >= 1:
          vq_wT = ctx.sb("vq_wT", [128, ND * 2 * D], BF16, sub)
          key_wT = ctx.sb("key_wT", [128, ND * D], BF16, sub)
          emit_transpose_w(ctx, vqw_hbm, 2 * D, D, vq_wT, wT_scr_f, wT_scr_b)
          emit_transpose_w(ctx, keyw_hbm, D, D, key_wT, wT_scr_f, wT_scr_b)
          kbrow = ctx.sb("kbrow", [1, D], F32, sub)
          p.dma(kbrow[0:1, :], keyb_row[0:1, :])
          kb_bc = ctx.sb("kb_bc", [128, D], F32, sub)
          for nch in range(2):
              p.mm(ctx.ps_c[:, 0:512], c["ones_f"][0:1, 0:128],
                   kbrow[0:1, nch * 512:(nch + 1) * 512], start=True, stop=True)
              p.scopy(kb_bc[:, nch * 512:(nch + 1) * 512], ctx.ps_c[:, 0:512])
          v_sb = ctx.sb("v_sb0", [128, 2 * D], BF16, sub)
          q_sb = ctx.sb("q_sb0", [128, 2 * D], BF16, sub)
          logk = ctx.sb("logk0", [128, 2 * D], F32, sub)
          omk = ctx.sb("omk0", [128, 2 * D], BF16, sub)
          zt = ctx.sb("zt0", [128, D], F32, sub)
          sc1 = ctx.sb("sc1_0", [128, D], F32, sub)
          expL = ctx.sb("expL0", [128, D], F32, sub)
          expNegL = expL
          A_bf = ctx.sb("A_bf0", [128, D], BF16, sub)
          B_bf = ctx.sb("B_bf0", [128, D], BF16, sub)
          AT_sb = ctx.sb("AT0", [128, 1024], BF16, sub)
          BT_sb = ctx.sb("BT0", [128, 1024], BF16, sub)
          ATm = ctx.sb("ATm0", [128, 1024], BF16, sub)
          Bm = ctx.sb("Bm0", [128, 1024], BF16, sub)
          PTm = ctx.sb("PTm0", [128, 128], BF16, sub)
          tot_sb = ctx.sb("tot0", [8, 1024], F32, sub)
          lamT = ctx.sb("lamT0", [128, 64], F32, sub)
          upd = ctx.sb("upd0", [128, 64], F32, sub)
          for sb_i in range(8):
              for tt in range(2):
                  i = sb_i * 2 + tt
                  for och in range(4):
                      for k in range(ND):
                          p.mm(ctx.ps_c[:, 0:512],
                               xnT[:, k * T + i * 128: k * T + (i + 1) * 128],
                               vq_wT[:, k * 2 * D + och * 512: k * 2 * D + (och + 1) * 512],
                               start=(k == 0), stop=(k == ND - 1))
                        
                      dst = v_sb if och < 2 else q_sb
                      oo = (och % 2)
                      p.scopy(dst[:, tt * D + oo * 512: tt * D + (oo + 1) * 512],
                              ctx.ps_c[:, 0:512])
                  for och in range(2):
                      for k in range(ND):
                          p.mm(ctx.ps_c[:, 0:512],
                               xnT[:, k * T + i * 128: k * T + (i + 1) * 128],
                               key_wT[:, k * D + och * 512: k * D + (och + 1) * 512],
                               start=(k == 0), stop=(k == ND - 1))
                      p.tt(zt[:, och * 512:(och + 1) * 512], ctx.ps_c[:, 0:512],
                           kb_bc[:, och * 512:(och + 1) * 512], ALU.add)
                  p.act(omk[:, tt * D:(tt + 1) * D], zt[:, :], AF.Sigmoid, scale=-1.0)
                  p.act(sc1[:, :], zt[:, :], AF.Abs)
                  p.act(sc1[:, :], sc1[:, :], AF.Exp, scale=-1.0)
                  p.act(sc1[:, :], sc1[:, :], AF.Ln, bias=1.0)
                  p.ts(zt[:, :], zt[:, :], 0.0, 0.0, ALU.min, ALU.add)
                  p.tt(zt[:, :], zt[:, :], sc1[:, :], ALU.subtract)
                  p.ts(logk[:, tt * D:(tt + 1) * D], zt[:, :], -5.0, 0.0, ALU.max, ALU.add)
              for g in range(2 if L0A_PHASE >= 2 else 0):
                  i = sb_i * 2 + g
                  gl = logk[:, g * D:(g + 1) * D]
                  for half in range(2):
                      p.mm(ctx.ps_L[:, half * 512:(half + 1) * 512],
                           c["blocktriT_f"][:, :], gl[:, half * 512:(half + 1) * 512],
                           start=True, stop=True)
                  p.act(expL[:, :], ctx.ps_L[:, 0:1024], AF.Exp)
                  p.tt(A_bf[:, :], q_sb[:, g * D:(g + 1) * D], expL[:, :], ALU.mult)
                  p.act(expNegL[:, :], ctx.ps_L[:, 0:1024], AF.Exp, scale=-1.0)
                  p.tt(B_bf[:, :], omk[:, g * D:(g + 1) * D], expNegL[:, :], ALU.mult)
                  for half in range(2):
                      p.mm(ctx.ps_c[0:8, 0:512], c["chunksum_f"][:, :],
                           gl[:, half * 512:(half + 1) * 512], start=True, stop=True)
                      p.scopy(tot_sb[0:8, half * 512:(half + 1) * 512], ctx.ps_c[0:8, 0:512])
                  for j in range(ND):
                      p.trans(ctx.ps_misc[:, 0:8], tot_sb[0:8, j * 128:(j + 1) * 128],
                              c["ident_f"][0:8, 0:8])
                      p.act(lamT[:, j * 8:(j + 1) * 8], ctx.ps_misc[:, 0:8], AF.Exp)
                  for j in range(ND):
                      p.trans(ctx.ps_tb[:, 0:128], A_bf[:, j * 128:(j + 1) * 128],
                              c["ident_bf"][:, :])
                      p.scopy(AT_sb[:, j * 128:(j + 1) * 128], ctx.ps_tb[:, 0:128])
                      p.trans(ctx.ps_tb[:, 0:128], B_bf[:, j * 128:(j + 1) * 128],
                              c["ident_bf"][:, :])
                      p.scopy(BT_sb[:, j * 128:(j + 1) * 128], ctx.ps_tb[:, 0:128])
                  for h in range(16 if L0A_SCANLEVEL >= 2 else 0):
                      j, r0 = h // 2, 64 * (h % 2)
                      p.mm(ctx.ps_misc[:, 0:128],
                           BT_sb[r0:r0 + 64, j * 128:(j + 1) * 128],
                           AT_sb[r0:r0 + 64, j * 128:(j + 1) * 128], start=True, stop=True)
                      p.tt(PTm[:, :], ctx.ps_misc[:, 0:128], c["blocktriT_f"][:, :], ALU.mult)
                      p.mm(ctx.ps_acc[:, h * 64:(h + 1) * 64], PTm[:, :],
                           v_sb[:, g * D + h * 64: g * D + (h + 1) * 64],
                           start=(h % 8 == 0), stop=(L0A_SCANLEVEL < 3), skip=True)
                  for cc in range(8 if L0A_SCANLEVEL >= 3 else 0):
                      p.tt(ATm[:, :], AT_sb[:, :],
                           c["gla_cmask"][:, cc * 1024:(cc + 1) * 1024], ALU.mult)
                      p.ts(Bm[:, :], B_bf[:, :], c["gla_rmask"][:, cc:cc + 1], None, ALU.mult)
                      for j in range(8):
                          p.mm(ctx.ps_acc[:, j * 128:(j + 1) * 128],
                               ATm[:, j * 128:(j + 1) * 128],
                               S_blk[:, j * 128:(j + 1) * 128],
                               start=False, stop=(cc == 7), skip=True)
                      for pr in range(8 if L0A_SCANLEVEL >= 4 else 0):
                          for hh in range(2):
                              h = 2 * pr + hh
                              p.mm(ctx.ps_pAB[0:64, hh * 64:(hh + 1) * 64],
                                   Bm[:, h * 64:(h + 1) * 64],
                                   v_sb[:, g * D + h * 64: g * D + (h + 1) * 64],
                                   start=True, stop=True, skip=True)
                          p.tt(upd[0:64, 0:64], ctx.ps_pAB[0:64, 0:64],
                               S_sb[0:64, pr * 64:(pr + 1) * 64], ALU.add)
                          p.tt(upd[64:128, 0:64], ctx.ps_pAB[0:64, 64:128],
                               S_sb[64:128, pr * 64:(pr + 1) * 64], ALU.add)
                          p.smul(S_sb[:, pr * 64:(pr + 1) * 64], upd[:, 0:64],
                                 lamT[:, pr * 8 + cc: pr * 8 + cc + 1])
                          p.vcopy(S_blk[0:64, pr * 128: pr * 128 + 64],
                                  S_sb[0:64, pr * 64:(pr + 1) * 64])
                          p.vcopy(S_blk[64:128, pr * 128 + 64: pr * 128 + 128],
                                  S_sb[64:128, pr * 64:(pr + 1) * 64])
                  if L0A_SCANLEVEL >= 2:
                      p.vcopy(h_sb[:, i * D:(i + 1) * D], ctx.ps_acc[:, 0:1024])

    with ctx.sub() as sub:
        out_wT = ctx.sb("out_wT", [128, ND * D], BF16, sub)
        emit_transpose_w(ctx, outw_hbm, D, D, out_wT, wT_scr_f, wT_scr_b)
        obrow = ctx.sb("obrow", [1, D], F32, sub)
        p.dma(obrow[0:1, :], outb_row[0:1, :])
        ob_bc = ctx.sb("ob_bc", [128, D], F32, sub)
        for nch in range(2):
            p.mm(ctx.ps_c[:, 0:512], c["ones_f"][0:1, 0:128],
                 obrow[0:1, nch * 512:(nch + 1) * 512], start=True, stop=True)
            p.scopy(ob_bc[:, nch * 512:(nch + 1) * 512], ctx.ps_c[:, 0:512])
        hT = ctx.sb("hT", [128, ND * T], BF16, sub)
        for i in range(NT):
            for j in range(ND):
                p.trans(ctx.ps_tb[:, 0:128], h_sb[:, i * D + j * 128: i * D + (j + 1) * 128],
                        c["ident_bf"][:, :])
                p.scopy(hT[:, j * T + i * 128: j * T + (i + 1) * 128], ctx.ps_tb[:, 0:128])
        xt = ctx.sb("xt_res0", [128, D], F32, sub)
        for i in range(NT):
            p.dma(xt[:, :], x_hbm[i * 128:(i + 1) * 128, :])
            for nchunk in range(2):
                cs = slice(nchunk * 512, (nchunk + 1) * 512)
                for k in range(ND):
                    p.mm(ctx.ps_c[:, 0:512],
                         hT[:, k * T + i * 128: k * T + (i + 1) * 128],
                         out_wT[:, k * D + nchunk * 512: k * D + (nchunk + 1) * 512],
                         start=(k == 0), stop=(k == ND - 1))
                p.tt(xt[:, cs], xt[:, cs], ob_bc[:, cs], ALU.add)
                p.tt(xt[:, cs], xt[:, cs], ctx.ps_c[:, 0:512], ALU.add)
            p.dma(xa_hbm[i * 128:(i + 1) * 128, :], xt[:, :], eng="gpsimd")

    with nc.Block() as block:
        ctx.finish(block)
    ctx.stack.close()
    return ctx


def run_l0a(x, key_w, key_b, vq_w, out_w, out_b, wr, consts, cores):
    ctx = build_l0a()
    maps = []
    for ci in cores:
        m = {"x": np.ascontiguousarray(x[ci]), "keyw": key_w,
             "keyb_row": key_b.reshape(1, -1), "vqw": vq_w, "outw": out_w,
             "outb_row": out_b.reshape(1, -1), "wr_row": wr.reshape(1, -1)}
        m.update(const_inmap(consts, L0A_CONSTS))
        maps.append(m)
    res = run_bass_kernel_spmd(ctx.nc, maps, core_ids=list(cores))
    return ([r["xa"] for r in res.results], [r["xnT_o"] for r in res.results],
            [r["keys_o"] for r in res.results])




L0F_CONSTS = ["ident_bf", "ident_f", "ones_f", "ones_bf"]


def build_l0f():
    ctx = Ctx("l0f")
    nc = ctx.nc
    xa_hbm = ctx.dram_in("xa", [T, D])
    xnT_hbm = ctx.dram_in("xnT_i", [128, ND * T], BF16)
    keys_hbm = ctx.dram_in("keys_i", [128, NT])
    br_hbm = ctx.dram_in("br", [1, 1])
    w1_hbm = ctx.dram_in("w1", [8, D, 4096])
    b1_hbm = ctx.dram_in("b1c", [8 * 4096, 1])
    w2_hbm = ctx.dram_in("w2", [8, 4096, D])
    b2_hbm = ctx.dram_in("b2c", [8 * D, 1])
    x1_hbm = ctx.dram_out("x1", [T, D])

    ctx.ps_c = ctx.ps("ps_c", [128, 512], F32)
    ctx.ps_h = ctx.ps("ps_h", [128, 512], F32)
    ctx.ps_bc = ctx.ps("ps_bc", [128, 512], F32)
    ctx.ps_tb = ctx.ps("ps_tb", [128, 128], BF16)
    ctx.ps_tf = ctx.ps("ps_tf", [128, 128], F32)

    p = ctx.start()
    ctx.load_consts(L0F_CONSTS)
    c = ctx.c

    keys = ctx.sb("keys", [128, NT], F32)
    p.dma(keys[:, :], keys_hbm)
    brs = ctx.sb("brs", [1, 1], F32)
    p.dma(brs[0:1, 0:1], br_hbm[0:1, 0:1])
    brc = ctx.sb("brc", [128, 1], F32)
    p.mm(ctx.ps_c[:, 0:1], c["ones_f"][0:1, 0:128], brs[0:1, 0:1], start=True, stop=True)
    p.scopy(brc[:, 0:1], ctx.ps_c[:, 0:1])
    p.ts(keys[:, :], keys[:, :], brc[:, 0:1], None, ALU.add)
    kmod = ctx.sb("kmod", [128, NT], F32)
    ki32 = ctx.sb("ki32", [128, NT], I32)
    p.ts(kmod[:, :], keys[:, :], -0.5, None, ALU.add)
    p.vcopy(ki32[:, :], kmod[:, :])
    p.ts(ki32[:, :], ki32[:, :], 7, None, ALU.bitwise_and)
    p.vcopy(kmod[:, :], ki32[:, :])
    gcol = ctx.sb("gcol", [128, 8], F32)
    b1c = ctx.sb("b1c_sb", [128, 32], F32)
    b2c = ctx.sb("b2c_sb", [128, ND], F32)
    mcol = b1c[:, 0:8]
    mrows_dram = ctx.nc.dram_tensor("mrows_scr", [8, T], BF16, kind="Internal").ap()
    m_st8 = ctx.sb("m_st8", [8, 128], BF16)
    for i in range(NT):
        for e in range(1, 8):
            p.ts(gcol[:, e:e + 1], kmod[:, i:i + 1], float(e), None, ALU.is_ge)
        p.ts(b1c[:, 0:1], gcol[:, 1:2], -1.0, -1.0, ALU.mult, ALU.subtract)
        for e in range(1, 7):
            p.tt(b1c[:, e:e + 1], gcol[:, e:e + 1], gcol[:, e + 1:e + 2], ALU.subtract)
        p.vcopy(b1c[:, 7:8], gcol[:, 7:8])
        p.trans(ctx.ps_tf[0:8, 0:128], mcol, c["ident_f"][:, :])
        p.scopy(m_st8[0:8, 0:128], ctx.ps_tf[0:8, 0:128])
        p.dma(mrows_dram[0:8, i * 128:(i + 1) * 128], m_st8[0:8, 0:128], eng="gpsimd")

    accT = ctx.sb("accT", [128, ND * T], BF16)
    p.memset(accT[:, :], 0.0)
    h_sb = ctx.sb("h_sb", [128, 32 * 512], BF16)
    xnTc = ctx.sb("xnTc", [128, ND * 512], BF16)
    w1b = ctx.sb("w1b", [128, ND * 4096], BF16)
    w2b = ctx.sb("w2b", [128, 32 * D], BF16)
    wstage = ctx.sb("wstage", [128, 512], F32)
    mbc = ctx.sb("mbc", [128, 512], BF16)
    mrow_st = ctx.sb("mrow_st", [1, 512], BF16)
    ytmp = wstage
    for e in range(8):
        for j in range(ND):
            for piece in range(8):
                p.dma(wstage[:, 0:512],
                      w1_hbm[e, j * 128:(j + 1) * 128, piece * 512:(piece + 1) * 512])
                p.vcopy(w1b[:, j * 4096 + piece * 512: j * 4096 + (piece + 1) * 512],
                        wstage[:, 0:512])
        for fk in range(32):
            for piece in range(2):
                p.dma(wstage[:, 0:512],
                      w2_hbm[e, fk * 128:(fk + 1) * 128, piece * 512:(piece + 1) * 512])
                p.vcopy(w2b[:, fk * D + piece * 512: fk * D + (piece + 1) * 512],
                        wstage[:, 0:512])
        for fm in range(32):
            p.dma(b1c[:, fm:fm + 1], b1_hbm[e * 4096 + fm * 128: e * 4096 + (fm + 1) * 128, 0:1])
        for dm in range(ND):
            p.dma(b2c[:, dm:dm + 1], b2_hbm[e * D + dm * 128: e * D + (dm + 1) * 128, 0:1])
        for tchunk in range(4):
            for j in range(ND):
                p.dma(xnTc[:, j * 512:(j + 1) * 512],
                      xnT_hbm[:, j * T + tchunk * 512: j * T + (tchunk + 1) * 512])
            p.dma(mrow_st[0:1, 0:512],
                  mrows_dram[e:e + 1, tchunk * 512:(tchunk + 1) * 512])
            p.mm(ctx.ps_bc[:, 0:512], c["ones_bf"][0:1, 0:128],
                 mrow_st[0:1, 0:512], start=True, stop=True)
            p.scopy(mbc[:, 0:512], ctx.ps_bc[:, 0:512])
            for fm in range(32):
                for j in range(ND):
                    p.mm(ctx.ps_h[:, 0:512],
                         w1b[:, j * 4096 + fm * 128: j * 4096 + (fm + 1) * 128],
                         xnTc[:, j * 512:(j + 1) * 512],
                         start=(j == 0), stop=(j == ND - 1))
                p.ts(h_sb[:, fm * 512:(fm + 1) * 512], ctx.ps_h[:, 0:512],
                     b1c[:, fm:fm + 1], 0.0, ALU.add, ALU.max)
            for dm in range(ND):
                for fk in range(32):
                    p.mm(ctx.ps_c[:, 0:512],
                         w2b[:, fk * D + dm * 128: fk * D + (dm + 1) * 128],
                         h_sb[:, fk * 512:(fk + 1) * 512],
                         start=(fk == 0), stop=(fk == 31))
                p.stt(ytmp[:, 0:512],
                      ctx.ps_c[:, 0:512], b2c[:, dm:dm + 1], mbc[:, 0:512],
                      ALU.add, ALU.mult)
                p.tt(accT[:, dm * T + tchunk * 512: dm * T + (tchunk + 1) * 512],
                     accT[:, dm * T + tchunk * 512: dm * T + (tchunk + 1) * 512],
                     ytmp[:, 0:512], ALU.add)
    xt = wstage
    for i in range(NT):
        for half in range(2):
            p.dma(xt[:, 0:512], xa_hbm[i * 128:(i + 1) * 128, half * 512:(half + 1) * 512])
            for jj in range(4):
                j = half * 4 + jj
                p.trans(ctx.ps_tb[:, 0:128], accT[:, j * T + i * 128: j * T + (i + 1) * 128],
                        c["ident_bf"][:, :])
                p.tt(xt[:, jj * 128:(jj + 1) * 128], xt[:, jj * 128:(jj + 1) * 128],
                     ctx.ps_tb[:, 0:128], ALU.add)
            p.dma(x1_hbm[i * 128:(i + 1) * 128, half * 512:(half + 1) * 512],
                  xt[:, 0:512], eng="gpsimd")

    with nc.Block() as block:
        ctx.finish(block)
    ctx.stack.close()
    return ctx


def run_l0f(xa_l, xnT_l, keys_l, br, w1, b1, w2, b2, consts, cores):
    ctx = build_l0f()
    maps = []
    for idx, ci in enumerate(cores):
        m = {"xa": xa_l[idx], "xnT_i": xnT_l[idx], "keys_i": keys_l[idx],
             "br": np.asarray(br, dtype=np.float32).reshape(1, 1),
             "w1": w1, "b1c": np.ascontiguousarray(b1.reshape(-1, 1)),
             "w2": w2, "b2c": np.ascontiguousarray(b2.reshape(-1, 1))}
        m.update(const_inmap(consts, L0F_CONSTS))
        maps.append(m)
    res = run_bass_kernel_spmd(ctx.nc, maps, core_ids=list(cores))
    return [r["x1"] for r in res.results]




L3A_CONSTS = ["ident_bf", "ident_f", "ones_f", "l3a_m01_f", "l3a_madd_f",
              "l3a_m01b_f", "l3a_maddb_f"]


def build_l3a():
    ctx = Ctx("l3a")
    nc = ctx.nc
    x_hbm = ctx.dram_in("x", [T, D])
    win_hbm = ctx.dram_in("win", [4112, D])
    wq_hbm = ctx.dram_in("wq", [D, D])
    bq_col = ctx.dram_in("bq_col", [D, 1])
    wk_hbm = ctx.dram_in("wk", [D, D])
    bk_col = ctx.dram_in("bk_col", [D, 1])
    wv_hbm = ctx.dram_in("wv", [D, D])
    bv_row = ctx.dram_in("bv_row", [1, D])
    wo_hbm = ctx.dram_in("wo3", [D, D])
    bo_row = ctx.dram_in("bo_row", [1, D])
    xa_hbm = ctx.dram_out("xa", [T, D])
    xnT_hbm = ctx.dram_out("xnT_o", [128, ND * T], BF16)
    gT_hbm = ctx.dram_out("gT_o", [16, T])

    ctx.ps_a = ctx.ps("ps_a", [128, 2048], F32)
    ctx.ps_c = ctx.ps("ps_c", [128, 512], F32)
    ctx.ps_tb = ctx.ps("ps_tb", [128, 128], BF16)
    ctx.ps_tf = ctx.ps("ps_tf", [128, 128], F32)
    ctx.ps_r = ctx.ps("ps_r", [128, 64], F32)

    p = ctx.start()
    ctx.load_consts(L3A_CONSTS)
    c = ctx.c

    rrhs = ctx.sb("rrhs", [128, ND * 16], F32)
    with ctx.sub() as rsub:
        rr_f = ctx.sb("rr_f", [16, D], F32, rsub)
        p.dma(rr_f[0:16, :], win_hbm[4096:4112, :])
        for j in range(ND):
            p.trans(ctx.ps_tf[:, 0:16], rr_f[0:16, j * 128:(j + 1) * 128],
                    c["ident_f"][0:16, 0:16])
            p.scopy(rrhs[:, j * 16:(j + 1) * 16], ctx.ps_tf[:, 0:16])

    xnT = ctx.sb("xnT_main", [128, ND * T], BF16)
    attn_oT = ctx.sb("attn_oT", [128, ND * T], BF16)
    sel_sub = ctx.sub()
    rlog = ctx.sb("rlog", [128, NT * 16], F32, sel_sub)
    st = ctx.sb("st_sel", [128, 8], F32, sel_sub)
    sc16 = ctx.sb("sc16", [128, 16], F32, sel_sub)
    ge16 = ctx.sb("ge16", [128, 16], F32, sel_sub)
    gT = ctx.sb("gT", [16, T], F32, sel_sub)
    emit_preamble(ctx, x_hbm, routers=[(rrhs, 16, None)], xnT=xnT,
                  router_dsts=[(rlog, 16, 0)])
    for j in range(ND):
        p.dma(xnT_hbm[:, j * T:(j + 1) * T], xnT[:, j * T:(j + 1) * T], eng="gpsimd")
    for i in range(NT):
        L = rlog[:, i * 16:(i + 1) * 16]
        p.red(st[:, 0:1], L, op="max")
        p.ts(ge16[:, :], L, st[:, 0:1], None, ALU.is_equal)
        p.stt(ge16[:, :], ge16[:, :], p.imm(-1000.0), L, ALU.mult, ALU.add)
        p.red(st[:, 1:2], ge16[:, :], op="max")
        p.ts(ge16[:, :], L, st[:, 1:2], None, ALU.is_ge)
        p.act(sc16[:, :], L, AF.Sigmoid)
        p.tt(sc16[:, :], sc16[:, :], ge16[:, :], ALU.mult)
        p.trans(ctx.ps_tf[0:16, 0:128], sc16[:, :], c["ident_f"][:, :])
        p.scopy(gT[0:16, i * 128:(i + 1) * 128], ctx.ps_tf[0:16, 0:128])
    p.dma(gT_hbm, gT[:, :], eng="gpsimd")
    sel_sub.close()

    with ctx.sub() as sub:
        QT = ctx.sb("QT", [128, ND * T], BF16, sub)
        KT = ctx.sb("KT", [128, ND * T], BF16, sub)
        V3 = ctx.sb("V3", [128, NT * D], BF16, sub)
        with ctx.sub() as sub2:
            wband_f = ctx.sb("wband_f3", [128, D], F32, sub2)
            wqb = [ctx.sb(f"wqb{k}", [128, D], BF16, sub2) for k in range(ND)]
            wkb = [ctx.sb(f"wkb{k}", [128, D], BF16, sub2) for k in range(ND)]
            for k in range(ND):
                emit_load_w_nat(ctx, wq_hbm, k, D, wband_f, wqb[k])
                emit_load_w_nat(ctx, wk_hbm, k, D, wband_f, wkb[k])
            bqc = ctx.sb("bqc", [128, 2 * ND], F32, sub2)
            for m in range(ND):
                p.dma(bqc[:, m:m + 1], bq_col[m * 128:(m + 1) * 128, 0:1])
                p.dma(bqc[:, ND + m:ND + m + 1], bk_col[m * 128:(m + 1) * 128, 0:1])
            for g, bands in ((0, wqb), (1, wkb)):
                dst = QT if g == 0 else KT
                for m in range(ND):
                    for tchunk in range(4):
                        for k in range(ND):
                            p.mm(ctx.ps_c[:, 0:512],
                                 bands[k][:, m * 128:(m + 1) * 128],
                                 xnT[:, k * T + tchunk * 512: k * T + (tchunk + 1) * 512],
                                 start=(k == 0), stop=(k == ND - 1))
                        p.ts(dst[:, m * T + tchunk * 512: m * T + (tchunk + 1) * 512],
                             ctx.ps_c[:, 0:512], bqc[:, g * ND + m: g * ND + m + 1],
                             None, ALU.add)
        with ctx.sub() as sub2:
            wband_f = ctx.sb("wband_f3v", [128, D], F32, sub2)
            wvb = [ctx.sb(f"wvb{k}", [128, D], BF16, sub2) for k in range(ND)]
            for k in range(ND):
                emit_load_w_nat(ctx, wv_hbm, k, D, wband_f, wvb[k])
            brow = ctx.sb("brow3", [1, D], F32, sub2)
            p.dma(brow[0:1, :], bv_row[0:1, :])
            vb_bc = ctx.sb("vb_bc3", [128, D], F32, sub2)
            for nch in range(2):
                p.mm(ctx.ps_c[:, 0:512], c["ones_f"][0:1, 0:128],
                     brow[0:1, nch * 512:(nch + 1) * 512], start=True, stop=True)
                p.scopy(vb_bc[:, nch * 512:(nch + 1) * 512], ctx.ps_c[:, 0:512])
            for i in range(NT):
                for nchunk in range(2):
                    for k in range(ND):
                        p.mm(ctx.ps_c[:, 0:512],
                             xnT[:, k * T + i * 128: k * T + (i + 1) * 128],
                             wvb[k][:, nchunk * 512:(nchunk + 1) * 512],
                             start=(k == 0), stop=(k == ND - 1))
                    p.tt(V3[:, i * D + nchunk * 512: i * D + (nchunk + 1) * 512],
                         ctx.ps_c[:, 0:512],
                         vb_bc[:, nchunk * 512:(nchunk + 1) * 512], ALU.add)
        P_sb = ctx.sb("P_sb3", [128, 2048], BF16, sub)
        PT_sb = ctx.sb("PT_sb3", [128, 128], BF16, sub)
        stat = ctx.sb("stat3", [128, 8], F32, sub)
        scr = ctx.sb("scr3", [128, 128], F32, sub)
        sc = 1.0 / 32.0
        for i in range(NT):
            if i > 0:
                p.memset(ctx.ps_a[:, 0:i * 128], -3.2e10)
            k0 = i * 128
            while k0 < 2048:
                n = min(512 - (k0 % 512) if k0 % 512 else 512, 2048 - k0)
                for j in range(ND):
                    p.mm(ctx.ps_a[:, k0:k0 + n],
                         QT[:, j * T + i * 128: j * T + (i + 1) * 128],
                         KT[:, j * T + k0: j * T + k0 + n],
                         start=(j == 0), stop=(j == ND - 1))
                k0 += n
            p.tt(scr[:, :], ctx.ps_a[:, i * 128:(i + 1) * 128], c["l3a_m01_f"][:, :], ALU.mult)
            p.tt(ctx.ps_a[:, i * 128:(i + 1) * 128], scr[:, :], c["l3a_madd_f"][:, :], ALU.add)
            if i + 1 < NT:
                p.tt(scr[:, :], ctx.ps_a[:, (i + 1) * 128:(i + 2) * 128],
                     c["l3a_m01b_f"][:, :], ALU.mult)
                p.tt(ctx.ps_a[:, (i + 1) * 128:(i + 2) * 128], scr[:, :],
                     c["l3a_maddb_f"][:, :], ALU.add)
            p.red(stat[:, 0:1], ctx.ps_a[:, 0:2048], op="max", negate=True)
            p.act(stat[:, 1:2], stat[:, 0:1], AF.Copy, scale=sc)
            p.act(P_sb[:, 0:2048], ctx.ps_a[:, 0:2048], AF.Exp, bias=stat[:, 1:2], scale=sc)
            p.red(stat[:, 2:3], P_sb[:, 0:2048], op="sum")
            p.recip(stat[:, 3:4], stat[:, 2:3])
            p.ts(P_sb[:, 0:2048], P_sb[:, 0:2048], stat[:, 3:4], None, ALU.mult)
            for kt in range(NT):
                p.trans(ctx.ps_tb[:, 0:128], P_sb[:, kt * 128:(kt + 1) * 128],
                        c["ident_bf"][:, :])
                p.scopy(PT_sb[:, 0:128], ctx.ps_tb[:, 0:128])
                for eb in range(ND):
                    p.mm(ctx.ps_a[:, eb * 128:(eb + 1) * 128],
                         V3[:, kt * D + eb * 128:kt * D + (eb + 1) * 128],
                         PT_sb[:, 0:128],
                         start=(kt == 0 and eb % 4 == 0), stop=(kt == NT - 1), skip=True)
            for eb in range(ND):
                p.scopy(attn_oT[:, eb * T + i * 128: eb * T + (i + 1) * 128],
                        ctx.ps_a[:, eb * 128:(eb + 1) * 128])

    wo_f = ctx.sb("wo_f3", [128, D], F32)
    wo_b = [ctx.sb(f"wo3_b{k}", [128, D], BF16) for k in range(ND)]
    for k in range(ND):
        emit_load_w_nat(ctx, wo_hbm, k, D, wo_f, wo_b[k])
    obrow = ctx.sb("obrow3", [1, D], F32)
    p.dma(obrow[0:1, :], bo_row[0:1, :])
    ob_bc = ctx.sb("ob_bc3", [128, D], F32)
    for nch in range(2):
        p.mm(ctx.ps_c[:, 0:512], c["ones_f"][0:1, 0:128],
             obrow[0:1, nch * 512:(nch + 1) * 512], start=True, stop=True)
        p.scopy(ob_bc[:, nch * 512:(nch + 1) * 512], ctx.ps_c[:, 0:512])
    xt = ctx.sb("xt_res3", [128, D], F32)
    for i in range(NT):
        p.dma(xt[:, :], x_hbm[i * 128:(i + 1) * 128, :])
        for nchunk in range(2):
            cs = slice(nchunk * 512, (nchunk + 1) * 512)
            for k in range(ND):
                p.mm(ctx.ps_c[:, 0:512],
                     attn_oT[:, k * T + i * 128: k * T + (i + 1) * 128],
                     wo_b[k][:, cs], start=(k == 0), stop=(k == ND - 1))
            p.tt(xt[:, cs], xt[:, cs], ob_bc[:, cs], ALU.add)
            p.tt(xt[:, cs], xt[:, cs], ctx.ps_c[:, 0:512], ALU.add)
        p.dma(xa_hbm[i * 128:(i + 1) * 128, :], xt[:, :], eng="gpsimd")

    with nc.Block() as block:
        ctx.finish(block)
    ctx.stack.close()
    return ctx


def run_l3a(x, win, wq, bq, wk, bk, wv, bv, wo, bo, consts, cores):
    ctx = build_l3a()
    maps = []
    for ci in cores:
        m = {"x": np.ascontiguousarray(x[ci]), "win": win, "wq": wq,
             "bq_col": bq.reshape(-1, 1), "wk": wk, "bk_col": bk.reshape(-1, 1),
             "wv": wv, "bv_row": bv.reshape(1, -1), "wo3": wo,
             "bo_row": bo.reshape(1, -1)}
        m.update(const_inmap(consts, L3A_CONSTS))
        maps.append(m)
    res = run_bass_kernel_spmd(ctx.nc, maps, core_ids=list(cores))
    return ([r["xa"] for r in res.results], [r["xnT_o"] for r in res.results],
            [r["gT_o"] for r in res.results])




L3F_CONSTS = ["ident_bf", "ident_f", "ones_f", "tri01_bf",
              "ropecosT_f", "ropesinT_f"]


def build_l3f():
    ctx = Ctx("l3f")
    nc = ctx.nc
    xa_hbm = ctx.dram_in("xa3", [T, D])
    xnT_hbm = ctx.dram_in("xnT_i", [128, ND * T], BF16)
    gT_hbm = ctx.dram_in("gT_i", [16, T])
    win_hbm = ctx.dram_in("win", [4112, D])
    wout_hbm = ctx.dram_in("wout", [D, 2 * D])
    kf_hbm = ctx.dram_in("kf", [16, 8, 256, 128])
    vf_hbm = ctx.dram_in("vf", [16, 8, 256, 128])
    x4_hbm = ctx.dram_out("x4", [T, D])

    qfT_d = nc.dram_tensor("qfT_scr", [128, ND * T], BF16, kind="Internal").ap()
    qaT_d = nc.dram_tensor("qaT_scr", [128, ND * T], BF16, kind="Internal").ap()
    kaT_d = nc.dram_tensor("kaT_scr", [128, ND * T], BF16, kind="Internal").ap()
    v_d = nc.dram_tensor("v3f_scr", [128, NT * D], BF16, kind="Internal").ap()

    ctx.ps_a = ctx.ps("ps_a", [128, 2048], F32)
    ctx.ps_b = ctx.ps("ps_b", [128, 512], F32)
    ctx.ps_c = ctx.ps("ps_c", [128, 512], F32)
    ctx.ps_tb = ctx.ps("ps_tb", [128, 128], BF16)
    ctx.ps_tf = ctx.ps("ps_tf", [128, 128], F32)

    p = ctx.start()
    ctx.load_consts(L3F_CONSTS)
    c = ctx.c

    attn3T = ctx.sb("attn3T", [128, ND * T], BF16)
    ffwdT = ctx.sb("ffwdT", [128, ND * T], BF16)
    p.memset(ffwdT[:, :], 0.0)

    with ctx.sub() as sub:
        xnT = ctx.sb("xnT3", [128, ND * T], BF16, sub)
        for j in range(ND):
            p.dma(xnT[:, j * T:(j + 1) * T], xnT_hbm[:, j * T:(j + 1) * T])
        w_inT = ctx.sb("w_inT", [128, ND * 4096], BF16, sub)
        wts_f = ctx.sb("wts_f", [128, D], F32, sub)
        wts_b = ctx.sb("wts_b", [128, D], BF16, sub)
        emit_transpose_w(ctx, win_hbm, 4096, D, w_inT, wts_f, wts_b)
        tmp = ctx.sb("projtmp", [128, T], BF16, sub)
        for g, dst in ((0, qfT_d), (1, qaT_d), (2, kaT_d)):
            for m in range(ND):
                o0 = g * ND + m
                for tchunk in range(4):
                    for k in range(ND):
                        p.mm(ctx.ps_c[:, 0:512],
                             w_inT[:, k * 4096 + o0 * 128:k * 4096 + (o0 + 1) * 128],
                             xnT[:, k * T + tchunk * 512: k * T + (tchunk + 1) * 512],
                             start=(k == 0), stop=(k == ND - 1))
                    p.scopy(tmp[:, tchunk * 512:(tchunk + 1) * 512], ctx.ps_c[:, 0:512])
                p.dma(dst[:, m * T:(m + 1) * T], tmp[:, 0:T], eng="gpsimd")
        for i in range(NT):
            for nchunk in range(2):
                for k in range(ND):
                    p.mm(ctx.ps_c[:, 0:512],
                         xnT[:, k * T + i * 128: k * T + (i + 1) * 128],
                         w_inT[:, k * 4096 + 3 * D + nchunk * 512:
                               k * 4096 + 3 * D + (nchunk + 1) * 512],
                         start=(k == 0), stop=(k == ND - 1))
                p.scopy(tmp[:, nchunk * 512:(nchunk + 1) * 512], ctx.ps_c[:, 0:512])
            p.dma(v_d[:, i * D:(i + 1) * D], tmp[:, 0:D], eng="gpsimd")

    with ctx.sub() as sub:
        qT = ctx.sb("qT3", [128, ND * T], BF16, sub)
        kT = ctx.sb("kT3", [128, ND * T], BF16, sub)
        V3 = ctx.sb("V3f", [128, NT * D], BF16, sub)
        for j in range(ND):
            p.dma(qT[:, j * T:(j + 1) * T], qaT_d[:, j * T:(j + 1) * T])
            p.dma(kT[:, j * T:(j + 1) * T], kaT_d[:, j * T:(j + 1) * T])
        for i in range(NT):
            p.dma(V3[:, i * D:(i + 1) * D], v_d[:, i * D:(i + 1) * D])
        s1 = ctx.sb("ropes1", [64, T], BF16, sub)
        s2 = ctx.sb("ropes2", [64, T], BF16, sub)
        q2c = ctx.sb("ropes3", [64, T], BF16, sub)
        for buf in (qT, kT):
            for h in range(ND):
                q1 = buf[0:64, h * T:(h + 1) * T]
                q2 = buf[64:128, h * T:(h + 1) * T]
                cosT, sinT = c["ropecosT_f"], c["ropesinT_f"]
                p.vcopy(q2c[:, :], q2)
                p.tt(s1[:, :], q1, cosT[:, :], ALU.mult)
                p.tt(s2[:, :], q2c[:, :], sinT[:, :], ALU.mult)
                p.tt(s1[:, :], s1[:, :], s2[:, :], ALU.subtract)
                p.tt(s2[:, :], q1, sinT[:, :], ALU.mult)
                p.vcopy(q1, s1[:, :])
                p.tt(s1[:, :], q2c[:, :], cosT[:, :], ALU.mult)
                p.tt(s2[:, :], s2[:, :], s1[:, :], ALU.add)
                p.vcopy(q2, s2[:, :])
        P_sb = ctx.sb("P_sb3f", [128, 2048], BF16, sub)
        PT_sb = ctx.sb("PT_sb3f", [128, 128], BF16, sub)
        stat = ctx.sb("stat3f", [128, 8], F32, sub)
        sc = 1.0 / np.sqrt(np.float32(128))
        for h in range(ND):
            for i in range(NT):
                klen = (i + 1) * 128
                for k0 in range(0, klen, 512):
                    n = min(512, klen - k0)
                    p.mm(ctx.ps_a[:, k0:k0 + n],
                         qT[:, h * T + i * 128: h * T + (i + 1) * 128],
                         kT[:, h * T + k0: h * T + k0 + n], start=True, stop=True)
                p.red(stat[:, 0:1], ctx.ps_a[:, 0:klen], op="max", negate=True)
                p.act(stat[:, 1:2], stat[:, 0:1], AF.Copy, scale=sc)
                p.act(P_sb[:, 0:klen], ctx.ps_a[:, 0:klen], AF.Exp, bias=stat[:, 1:2], scale=sc)
                p.tt(P_sb[:, i * 128:klen], P_sb[:, i * 128:klen], c["tri01_bf"][:, :], ALU.mult)
                p.red(stat[:, 2:3], P_sb[:, 0:klen], op="sum")
                p.recip(stat[:, 3:4], stat[:, 2:3])
                p.ts(P_sb[:, 0:klen], P_sb[:, 0:klen], stat[:, 3:4], None, ALU.mult)
                for kt in range(i + 1):
                    p.trans(ctx.ps_tb[:, 0:128], P_sb[:, kt * 128:(kt + 1) * 128],
                            c["ident_bf"][:, :])
                    p.scopy(PT_sb[:, 0:128], ctx.ps_tb[:, 0:128])
                    p.mm(ctx.ps_b[:, 0:128],
                         V3[:, kt * D + h * 128: kt * D + (h + 1) * 128],
                         PT_sb[:, 0:128], start=(kt == 0), stop=(kt == i))
                p.scopy(attn3T[:, h * T + i * 128: h * T + (i + 1) * 128],
                        ctx.ps_b[:, 0:128])

    with ctx.sub() as sub:
        qfT = ctx.sb("qfT3", [128, ND * T], BF16, sub)
        for j in range(ND):
            p.dma(qfT[:, j * T:(j + 1) * T], qfT_d[:, j * T:(j + 1) * T])
        keT = ctx.sb("keT", [128, 256], BF16, sub)
        veb = ctx.sb("veb", [128, 256], BF16, sub)
        kstage = ctx.sb("kstage", [128, 128], F32, sub)
        kstage_b = ctx.sb("kstage_b", [128, 128], BF16, sub)
        grow = ctx.sb("grow", [1, 512], F32, sub)
        gbc = ctx.sb("gbc", [128, 512], F32, sub)
        aT = ctx.sb("aT3", [128, 1024], BF16, sub)
        otmp = ctx.sb("otmp", [128, 512], BF16, sub)
        for e in range(16):
            for h in range(ND):
                for s2_ in range(2):
                    p.dma(kstage[:, :], kf_hbm[e, h, s2_ * 128:(s2_ + 1) * 128, :])
                    p.vcopy(kstage_b[:, :], kstage[:, :])
                    p.trans(ctx.ps_tb[:, 0:128], kstage_b[:, :], c["ident_bf"][:, :])
                    p.scopy(keT[:, s2_ * 128:(s2_ + 1) * 128], ctx.ps_tb[:, 0:128])
                    p.dma(kstage[:, :], vf_hbm[e, h, s2_ * 128:(s2_ + 1) * 128, :])
                    p.vcopy(veb[:, s2_ * 128:(s2_ + 1) * 128], kstage[:, :])
                for tchunk in range(4):
                    tc = slice(tchunk * 512, (tchunk + 1) * 512)
                    if h == 0:
                        pass
                    p.dma(grow[0:1, 0:512], gT_hbm[e:e + 1, tc])
                    p.mm(ctx.ps_c[:, 0:512], c["ones_f"][0:1, 0:128], grow[0:1, 0:512],
                         start=True, stop=True)
                    p.scopy(gbc[:, 0:512], ctx.ps_c[:, 0:512])
                    for s2_ in range(2):
                        p.mm(ctx.ps_a[:, s2_ * 512:(s2_ + 1) * 512],
                             keT[:, s2_ * 128:(s2_ + 1) * 128],
                             qfT[:, h * T + tchunk * 512: h * T + (tchunk + 1) * 512],
                             start=True, stop=True)
                        p.act(aT[:, s2_ * 512:(s2_ + 1) * 512],
                              ctx.ps_a[:, s2_ * 512:(s2_ + 1) * 512], AF.Gelu)
                    for s2_ in range(2):
                        p.mm(ctx.ps_b[:, 0:512], veb[:, s2_ * 128:(s2_ + 1) * 128],
                             aT[:, s2_ * 512:(s2_ + 1) * 512],
                             start=(s2_ == 0), stop=(s2_ == 1))
                    p.tt(otmp[:, 0:512], ctx.ps_b[:, 0:512], gbc[:, 0:512], ALU.mult)
                    p.tt(ffwdT[:, h * T + tchunk * 512: h * T + (tchunk + 1) * 512],
                         ffwdT[:, h * T + tchunk * 512: h * T + (tchunk + 1) * 512],
                         otmp[:, 0:512], ALU.add)

    with ctx.sub() as sub:
        w_outT = ctx.sb("w_outT", [128, 16 * D], BF16, sub)
        wts_f = ctx.sb("wts_f4", [128, 2 * D], F32, sub)
        wts_b = ctx.sb("wts_b4", [128, 2 * D], BF16, sub)
        emit_transpose_w(ctx, wout_hbm, D, 2 * D, w_outT, wts_f, wts_b)
        xt = ctx.sb("xt_res4", [128, D], F32, sub)
        for i in range(NT):
            p.dma(xt[:, :], xa_hbm[i * 128:(i + 1) * 128, :])
            for nchunk in range(2):
                cs = slice(nchunk * 512, (nchunk + 1) * 512)
                for k in range(16):
                    src_ = attn3T if k < ND else ffwdT
                    kk = k % ND
                    p.mm(ctx.ps_c[:, 0:512],
                         src_[:, kk * T + i * 128: kk * T + (i + 1) * 128],
                         w_outT[:, k * D + nchunk * 512: k * D + (nchunk + 1) * 512],
                         start=(k == 0), stop=(k == 15))
                p.tt(xt[:, cs], xt[:, cs], ctx.ps_c[:, 0:512], ALU.add)
            p.dma(x4_hbm[i * 128:(i + 1) * 128, :], xt[:, :], eng="gpsimd")

    with nc.Block() as block:
        ctx.finish(block)
    ctx.stack.close()
    return ctx


def run_l3f(xa_l, xnT_l, gT_l, win, wout, kf, vf, consts, cores):
    ctx = build_l3f()
    maps = []
    for idx, ci in enumerate(cores):
        m = {"xa3": xa_l[idx], "xnT_i": xnT_l[idx], "gT_i": gT_l[idx],
             "win": win, "wout": wout, "kf": kf, "vf": vf}
        m.update(const_inmap(consts, L3F_CONSTS))
        maps.append(m)
    res = run_bass_kernel_spmd(ctx.nc, maps, core_ids=list(cores))
    return [r["x4"] for r in res.results]


L1A_CONSTS = ["ident_bf", "ident_f", "tri01_bf", "ones_bf", "ones_f"]
L1F_CONSTS = ["ident_bf", "ones_f"]


def run_l1a(x, W_attn, b_attn, W_proj, b_proj, rw0, rw1, consts, cores):
    ctx = build_l1a()
    maps = []
    for ci in cores:
        m = {"x": np.ascontiguousarray(x[ci]), "wattn": W_attn,
             "battn_row": b_attn.reshape(1, -1), "battn_col": b_attn.reshape(-1, 1),
             "wproj": W_proj, "bproj_row": b_proj.reshape(1, -1),
             "rw0": rw0, "rw1": rw1}
        m.update(const_inmap(consts, L1A_CONSTS))
        maps.append(m)
    res = run_bass_kernel_spmd(ctx.nc, maps, core_ids=list(cores))
    return ([r["xa"] for r in res.results], [r["xnT_o"] for r in res.results],
            [r["wselT_o"] for r in res.results])


def run_l1f(xa_l, xnT_l, wselT_l, wg0, wu0, wd0, wg1, wu1, wd1, consts, cores):
    ctx = build_l1f()
    maps = []
    for idx, ci in enumerate(cores):
        m = {"xa": xa_l[idx], "xnT_i": xnT_l[idx], "wselT_i": wselT_l[idx],
             "wg0": wg0, "wu0": wu0, "wd0": wd0, "wg1": wg1, "wu1": wu1, "wd1": wd1}
        m.update(const_inmap(consts, L1F_CONSTS))
        maps.append(m)
    res = run_bass_kernel_spmd(ctx.nc, maps, core_ids=list(cores))
    return [r["x2"] for r in res.results]


_CONSTS = None


def kernel(**inputs):
    global _CONSTS
    if _CONSTS is None:
        _CONSTS = make_consts()
    consts = _CONSTS
    inp = {k: np.ascontiguousarray(np.asarray(v, dtype=np.float32))
           for k, v in inputs.items()}
    cores = list(range(8))
    x = inp["L0_attn_x"]

    xa_l, xnT_l, keys_l = run_l0a(x, inp["L0_attn_key_w"], inp["L0_attn_key_b"],
                                  inp["L0_attn_vq_w"], inp["L0_attn_out_w"],
                                  inp["L0_attn_out_b"], inp["L0_ffn_wr"],
                                  consts, cores)
    x1_l = run_l0f(xa_l, xnT_l, keys_l, inp["L0_ffn_br"], inp["L0_ffn_W1"],
                   inp["L0_ffn_b1"], inp["L0_ffn_W2"], inp["L0_ffn_b2"],
                   consts, cores)
    xa_l, xnT_l, wselT_l = run_l1a(x1_l, inp["L1_attn_W_attn"], inp["L1_attn_b_attn"],
                                   inp["L1_attn_W_proj"], inp["L1_attn_b_proj"],
                                   inp["L1_ffn_router_w0"], inp["L1_ffn_router_w1"],
                                   consts, cores)
    x2_l = run_l1f(xa_l, xnT_l, wselT_l, inp["L1_ffn_wg0"], inp["L1_ffn_wu0"],
                   inp["L1_ffn_wd0"], inp["L1_ffn_wg1"], inp["L1_ffn_wu1"],
                   inp["L1_ffn_wd1"], consts, cores)
    x3 = run_l2(x2_l, inp["L2_attn_Wqvk"], inp["L2_attn_Wo"], inp["L2_ffn_weights"],
                inp["L2_ffn_scale"].reshape(1, -1), inp["L2_ffn_bias"].reshape(1, -1),
                consts, cores)
    xa_l, xnT_l, gT_l = run_l3a(x3, inp["L3_ffn_w_in"], inp["L3_attn_wq"],
                                inp["L3_attn_bq"], inp["L3_attn_wk"], inp["L3_attn_bk"],
                                inp["L3_attn_wv"], inp["L3_attn_bv"], inp["L3_attn_wo"],
                                inp["L3_attn_bo"], consts, cores)
    x4_l = run_l3f(xa_l, xnT_l, gT_l, inp["L3_ffn_w_in"], inp["L3_ffn_w_out"],
                   inp["L3_ffn_k_ffwd"], inp["L3_ffn_v_ffwd"], consts, cores)
    return np.stack(x4_l).astype(np.float32)
